# version 06
# baseline (speedup 1.0000x reference)
import math
from collections import defaultdict
import numpy as np
import concourse.bass as bass
import concourse.mybir as mybir
from concourse.ap import AP
from concourse.bass_utils import run_bass_kernel_spmd

F32 = mybir.dt.float32
BF16 = mybir.dt.bfloat16
AF = mybir.ActivationFunctionType
ALU = mybir.AluOpType
AX = mybir.AxisListType

S, D, H, DK, DH = 2048, 1024, 16, 64, 4096
NSEQ = 2
NTT = S // 128
NFT = D // 128
QTB = 512
NQTB = S // QTB
NKT = S // 128
CH = 1024
NCH = S // CH
EPS = 1e-5
SH_W = 1152
SH_OFF = 1408

COMPUTE_ENGINES = ("pe", "act", "dve")
DMA_ENGINES = ("sp", "pl")
ENGINES = COMPUTE_ENGINES + DMA_ENGINES
NSEM_DMA = 6


class Emitter:
    def __init__(self):
        self.prog = {e: [] for e in ENGINES}
        self.ndma = {e: 0 for e in DMA_ENGINES}
        self._pending_gate = {}

    def _take_gate(self, eng):
        return self._pending_gate.pop(eng, [])

    def op(self, eng, fn, deps=()):
        waits = self._take_gate(eng) + [d for d in deps if d is not None]
        self.prog[eng].append({"k": "c", "fn": fn, "waits": waits,
                               "marked": False})
        return ("c", eng, len(self.prog[eng]) - 1)

    def fence(self, eng, deps=()):
        return self.op(eng, lambda e: e.drain(), deps=deps)

    def dma(self, eng, out_ap, in_ap, deps=(), slow=False, accum=False):
        i = self.ndma[eng]
        self.ndma[eng] += 1
        slot, k = i % NSEM_DMA, i // NSEM_DMA

        def fn(e, o=out_ap, i_=in_ap, s=slow, a=accum):
            if a:
                return e.dma_start(out=o, in_=i_, accum_op=ALU.add,
                                   allow_slow_non_contiguous=s)
            return e.dma_start(out=o, in_=i_, allow_slow_non_contiguous=s)

        waits = self._take_gate(eng) + [d for d in deps if d is not None]
        self.prog[eng].append({"k": "d", "fn": fn, "waits": waits,
                               "slot": slot, "seq": k, "marked": False})
        return ("d", eng, slot, k)

    def barrier(self):
        evs = []
        for e in COMPUTE_ENGINES:
            if self.prog[e]:
                evs.append(("c", e, len(self.prog[e]) - 1))
        for e in DMA_ENGINES:
            seen = {}
            for ent in reversed(self.prog[e]):
                if ent["k"] == "d" and ent["slot"] not in seen:
                    seen[ent["slot"]] = ent["seq"]
                    if len(seen) == NSEM_DMA:
                        break
            for slot, seq in seen.items():
                evs.append(("d", e, slot, seq))
        return evs

    def gate(self):
        evs = self.barrier()
        for e in ENGINES:
            for ev in evs:
                if ev[1] != e or (e in DMA_ENGINES and ev[0] == "d"):
                    self._pending_gate.setdefault(e, []).append(ev)
        return evs

    def finalize(self, sems_c, sems_d, final_waits=()):
        for e in ENGINES:
            for ent in self.prog[e]:
                for ev in ent["waits"]:
                    if ev[0] == "c":
                        self.prog[ev[1]][ev[2]]["marked"] = True
        for ev in final_waits:
            if ev[0] == "c":
                self.prog[ev[1]][ev[2]]["marked"] = True
        cum = {}
        for e in COMPUTE_ENGINES:
            c = 0
            cums = []
            for ent in self.prog[e]:
                if ent["marked"]:
                    c += 1
                cums.append(c)
            cum[e] = cums

        def make_body(e):
            def body(eng):
                waited = defaultdict(int)

                def do_wait(ev):
                    if ev[0] == "c":
                        _, pe_, idx = ev
                        v = cum[pe_][idx]
                        if waited[("c", pe_)] < v:
                            waited[("c", pe_)] = v
                            eng.wait_ge(sems_c[pe_], v)
                    else:
                        _, de, slot, k = ev
                        v = 16 * (k + 1)
                        if waited[("d", de, slot)] < v:
                            waited[("d", de, slot)] = v
                            eng.wait_ge(sems_d[de][slot], v)

                for ent in self.prog[e]:
                    for ev in ent["waits"]:
                        do_wait(ev)
                    if ent["k"] == "d":
                        if ent["seq"] > 0:
                            v = 16 * ent["seq"]
                            if waited[("d", e, ent["slot"])] < v:
                                waited[("d", e, ent["slot"])] = v
                                eng.wait_ge(sems_d[e][ent["slot"]], v)
                        ent["fn"](eng).then_inc(sems_d[e][ent["slot"]], 16)
                    else:
                        inst = ent["fn"](eng)
                        if ent["marked"]:
                            inst.then_inc(sems_c[e], 1)
                if e in DMA_ENGINES:
                    for ev in final_waits:
                        do_wait(ev)
            return body
        return {e: make_body(e) for e in ENGINES}


def mk(fname, *args, **kw):
    def fn(e):
        return getattr(e, fname)(*args, **kw)
    return fn


def build_program():
    nc = bass.Bass("TRN2", target_bir_lowering=False)

    x_d = nc.dram_tensor("x", [NSEQ, S, D], F32, kind="ExternalInput")
    wq_d = nc.dram_tensor("wqr", [128, NFT, D], F32, kind="ExternalInput")
    wk_d = nc.dram_tensor("wkr", [128, NFT, D], F32, kind="ExternalInput")
    wv_d = nc.dram_tensor("wvr", [128, NFT, D], F32, kind="ExternalInput")
    wo_d = nc.dram_tensor("wo_r", [128, NFT * D], F32, kind="ExternalInput")
    w1_d = nc.dram_tensor("w1tl", [128, 32, 8, 128], F32, kind="ExternalInput")
    w2_d = nc.dram_tensor("w2tl", [128, 32, 8, 128], F32, kind="ExternalInput")
    h1_d = nc.dram_tensor("h1tl", [128, 32, 32, 128], F32, kind="ExternalInput")
    h2_d = nc.dram_tensor("h2tl", [128, 32, 32, 128], F32, kind="ExternalInput")
    w3_d = nc.dram_tensor("w3r", [128, 2, 32, 512], F32, kind="ExternalInput")
    s1_d = nc.dram_tensor("s1r", [128, 32], F32, kind="ExternalInput")
    s2_d = nc.dram_tensor("s2r", [128, 32], F32, kind="ExternalInput")
    sh1_d = nc.dram_tensor("sh1r", [128, 32], F32, kind="ExternalInput")
    sh2_d = nc.dram_tensor("sh2r", [128, 32], F32, kind="ExternalInput")
    s3_d = nc.dram_tensor("s3row", [1, D], F32, kind="ExternalInput")
    rb_d = nc.dram_tensor("rb", [32, H], F32, kind="ExternalInput")
    oh_d = nc.dram_tensor("onehot", [32, 4095], F32, kind="ExternalInput")
    id_d = nc.dram_tensor("ident", [128, 128], F32, kind="ExternalInput")
    ind_d = nc.dram_tensor("indall", [128, NFT * H], F32, kind="ExternalInput")
    out_d = nc.dram_tensor("out", [NSEQ, S, D], F32, kind="ExternalOutput")
    bv_d = nc.dram_tensor("bv", [H, 4095], F32, kind="Internal")

    em = Emitter()
    from contextlib import ExitStack

    _uid = [0]

    def SB(name, shape, dtype):
        _uid[0] += 1
        return nc.sbuf_tensor(f"{name}_{_uid[0]}", shape, dtype)

    def PS(name, shape, dtype=F32):
        _uid[0] += 1
        return nc.psum_tensor(f"{name}_{_uid[0]}", shape, dtype)

    final_store_evs = []

    with ExitStack() as outer:
        sems_c = {e: outer.enter_context(nc.semaphore("sem_" + e))
                  for e in COMPUTE_ENGINES}
        sems_d = {e: [outer.enter_context(nc.semaphore(f"sem_{e}{i}"))
                      for i in range(NSEM_DMA)] for e in DMA_ENGINES}

        id_bf = outer.enter_context(SB("id_bf", [128, 128], BF16))
        ind_bf = outer.enter_context(SB("ind_bf", [128, NFT * H], BF16))
        s1_sb = outer.enter_context(SB("s1_sb", [128, 32], F32))
        s2_sb = outer.enter_context(SB("s2_sb", [128, 32], F32))
        sh1_sb = outer.enter_context(SB("sh1_sb", [128, 32], F32))
        sh2_sb = outer.enter_context(SB("sh2_sb", [128, 32], F32))
        s3b_sb = outer.enter_context(SB("s3b_sb", [128, D], F32))
        ones128 = outer.enter_context(SB("ones128", [1, 128], F32))
        ones64 = outer.enter_context(SB("ones64", [1, 64], F32))
        eps_col = outer.enter_context(SB("eps_col", [128, 1], F32))
        cprc = outer.enter_context(SB("cprc", [16, 1], F32))
        cfrc = outer.enter_context(SB("cfrc", [16, 1], F32))
        cmax_sb = outer.enter_context(SB("cmax_sb", [16, 1], F32))
        nrm_sb = outer.enter_context(SB("nrm_sb", [16, 12], F32))
        la_sb = outer.enter_context(SB("la_sb", [16, 1], F32))
        lb_sb = outer.enter_context(SB("lb_sb", [16, 1], F32))
        Mcol = outer.enter_context(SB("Mcol", [16, 1], F32))
        b0col = outer.enter_context(SB("b0col", [16, 1], F32))
        bPcol = outer.enter_context(SB("bPcol", [16, 1], F32))
        bFcol = outer.enter_context(SB("bFcol", [16, 1], F32))
        brow_sb = outer.enter_context(SB("brow_sb", [1, 5 * H], F32))
        biasP = outer.enter_context(SB("biasP", [128, H], F32))
        biasF = outer.enter_context(SB("biasF", [128, H], F32))
        bias0 = outer.enter_context(SB("bias0", [128, H], F32))
        biasPr = outer.enter_context(SB("biasPr", [128, H], F32))
        biasFr = outer.enter_context(SB("biasFr", [128, H], F32))

        ev_id = em.dma("pl", id_bf[:], id_d.ap())
        ev_ind = em.dma("pl", ind_bf[:], ind_d.ap())
        ev_s1 = em.dma("sp", s1_sb[:], s1_d.ap())
        ev_s2 = em.dma("sp", s2_sb[:], s2_d.ap())
        ev_sh1 = em.dma("sp", sh1_sb[:], sh1_d.ap())
        ev_sh2 = em.dma("sp", sh2_sb[:], sh2_d.ap())
        em.op("dve", mk("memset", ones128[:], 1.0))
        em.op("dve", mk("memset", ones64[:], 1.0))
        em.op("dve", mk("memset", eps_col[:], EPS))
        ev_ones = em.fence("dve")

        with SB("rb_sb", [32, H], F32) as rb_sb, \
             SB("oh_sb", [32, 4095], F32) as oh_sb, \
             SB("bv_sb", [16, 4096], F32) as bv_sb, \
             SB("s3r_sb", [1, D], F32) as s3r_sb, \
             PS("set_ps", [128, 1024], F32) as set_ps, \
             PS("bvp", [16, 512], F32) as bvp:
            ev_s3 = em.dma("sp", s3r_sb[:], s3_d.ap())
            ev_rb = em.dma("sp", rb_sb[:], rb_d.ap())
            ev_oh = em.dma("sp", oh_sb[:], oh_d.ap())
            ev_mmb = None
            for c in range(2):
                ev_mmb = em.op("pe", mk("matmul", set_ps[:, c * 512:(c + 1) * 512],
                                        ones128[:], s3r_sb[:, c * 512:(c + 1) * 512],
                                        start=True, stop=True),
                               deps=[ev_s3, ev_ones, ev_mmb])
            em.op("dve", mk("tensor_copy", out=s3b_sb[:], in_=set_ps[:]),
                  deps=[ev_mmb, ev_ones])
            ev_bve = None
            for c in range(8):
                n = min(512, 4095 - c * 512)
                ev_bvm = em.op("pe", mk("matmul", bvp[:16, 0:n],
                                        rb_sb[:], oh_sb[:, c * 512:c * 512 + n],
                                        start=True, stop=True),
                               deps=[ev_rb, ev_oh, ev_bve])
                ev_bve = em.op("dve", mk("tensor_copy",
                                         out=bv_sb[:16, c * 512:c * 512 + n],
                                         in_=bvp[:16, 0:n]), deps=[ev_bvm])
            ev_cm0 = em.op("dve", mk("reduce_max", out=cmax_sb[:],
                                     in_=bv_sb[:16, 0:4095], axis=AX.X),
                           deps=[ev_bve])
            ev_cmf = em.fence("dve")
            ev_bvd = em.dma("sp", bv_d.ap(), bv_sb[:16, 0:4095], deps=[ev_bve])
            ev_cp = em.dma("sp", cprc[:], AP(bv_d, 0, [[4095, 16], [1, 1]]),
                           deps=[ev_bvd], slow=True)
            ev_cf = em.dma("sp", cfrc[:], AP(bv_d, 4094, [[4095, 16], [1, 1]]),
                           deps=[ev_bvd], slow=True)
        em.gate()

        x_ap = x_d.ap()
        out_ap = out_d.ap()

        def emit_ln(tiles, nt, bufs, dst_fm, dst_stride):
            (xt, x2, m_sb, v_sb, mean_sb, msq_sb, var_sb, lnv_sb, r_sb,
             mb_sb, y_tm, tp_ps) = bufs
            ev_rd = [None, None]
            ev_x2r = None
            ev_x2w = None
            for t in range(nt):
                sl = t % 2
                ev_x = em.dma("sp", xt[sl][:], tiles[t], deps=ev_rd[sl] or [])
                ev_m = em.op("dve", mk("reduce_sum", out=m_sb[:, t:t + 1],
                                       in_=xt[sl][:], axis=AX.X), deps=[ev_x])
                ev_sq = em.op("act", mk("square", out=x2[:], in_=xt[sl][:]),
                              deps=[ev_x, ev_x2r, ev_x2w])
                ev_x2w = ev_sq
                ev_v = em.op("dve", mk("reduce_sum", out=v_sb[:, t:t + 1],
                                       in_=x2[:], axis=AX.X), deps=[ev_sq])
                ev_rd[sl] = [ev_m, ev_sq]
                ev_x2r = ev_v
            ev_f1 = em.fence("dve")
            ev_mean = em.op("dve", mk("tensor_scalar_mul", mean_sb[:], m_sb[:],
                                      1.0 / D), deps=[ev_f1])
            ev_vv = em.op("dve", mk("tensor_scalar_mul", var_sb[:], v_sb[:],
                                    1.0 / D), deps=[ev_mean])
            ev_msq = em.op("dve", mk("tensor_mul", msq_sb[:], mean_sb[:],
                                     mean_sb[:]), deps=[ev_mean])
            ev_var = em.op("dve", mk("tensor_sub", var_sb[:], var_sb[:],
                                     msq_sb[:]), deps=[ev_vv, ev_msq])
            ev_ln = em.op("act", mk("activation", out=lnv_sb[:], in_=var_sb[:],
                                    func=AF.Ln, bias=eps_col[:]),
                          deps=[ev_var, ev_ones])
            ev_r = em.op("act", mk("activation", out=r_sb[:], in_=lnv_sb[:],
                                   func=AF.Exp, scale=-0.5), deps=[ev_ln])
            ev_mb0 = em.op("dve", mk("tensor_mul", mb_sb[:], mean_sb[:],
                                     r_sb[:]), deps=[ev_r, ev_mean])
            ev_mb = em.op("dve", mk("tensor_scalar_mul", mb_sb[:], mb_sb[:],
                                    -1.0), deps=[ev_mb0])
            ev_app = [None, None]
            ev_tp = None
            ev_yev = None
            for t in range(nt):
                sl = t % 2
                ev_x = em.dma("sp", xt[sl][:], tiles[t],
                              deps=(ev_rd[sl] or []) + [ev_app[sl]])
                ev_rd[sl] = None
                ev_a = em.op("act", mk("activation", out=y_tm[:], in_=xt[sl][:],
                                       func=AF.Identity,
                                       bias=mb_sb[:, t:t + 1],
                                       scale=r_sb[:, t:t + 1]),
                             deps=[ev_x, ev_mb, ev_r, ev_tp])
                ev_app[sl] = ev_a
                for j in range(8):
                    ev_tp = em.op("pe", mk("transpose",
                                           tp_ps[:, j * 128:(j + 1) * 128],
                                           y_tm[:, j * 128:(j + 1) * 128],
                                           id_bf[:]),
                                  deps=[ev_a, ev_id, ev_yev])
                ev_yev = em.op("dve", mk(
                    "tensor_copy",
                    out=AP(dst_fm, t * 128,
                           [[NFT * dst_stride, 128], [dst_stride, 8], [1, 128]]),
                    in_=AP(tp_ps, 0, [[1024, 128], [128, 8], [1, 128]])),
                    deps=[ev_tp])
            return ev_yev

        for sq in range(NSEQ):
            x_tiles = x_ap[sq].rearrange("(t p) d -> t p d", p=128)
            out_tiles = out_ap[sq].rearrange("(t p) d -> t p d", p=128)

            with ExitStack() as attn:
                Q_sb = attn.enter_context(SB("Q_sb", [128, NFT * S], BF16))
                K_sb = attn.enter_context(SB("K_sb", [128, NFT * S], BF16))
                V_sb = attn.enter_context(SB("V_sb", [128, NKT * H * 65], BF16))

                with ExitStack() as ab:
                    y_fm = ab.enter_context(SB("y_fm", [128, NFT * S], BF16))
                    ws0 = ab.enter_context(SB("ws0", [128, NFT * D], BF16))
                    ws1 = ab.enter_context(SB("ws1", [128, NFT * D], BF16))
                    xt = [ab.enter_context(SB(f"xt{i}", [128, D], F32))
                          for i in range(2)]
                    x2 = ab.enter_context(SB("x2", [128, D], F32))
                    y_tm = ab.enter_context(SB("y_tm", [128, D], BF16))
                    lnb = [ab.enter_context(SB(f"lnb{i}", [128, NTT], F32))
                           for i in range(8)]
                    nsq = ab.enter_context(SB("nsq", [128, 512], BF16))

                    with PS("tp_ps", [128, 1024], BF16) as tp_ps, \
                         PS("qkv_ps", [128, S], F32) as qkv_ps:
                        ev_yev = emit_ln(
                            x_tiles, NTT,
                            (xt, x2, lnb[0], lnb[1], lnb[2], lnb[3], lnb[4],
                             lnb[5], lnb[6], lnb[7], y_tm, tp_ps),
                            y_fm, S)

                        ev_wq = em.dma("pl", ws0[:],
                                       AP(wq_d, 0, [[NFT * D, 128], [1, NFT * D]]))
                        ev_wk = em.dma("pl", ws1[:],
                                       AP(wk_d, 0, [[NFT * D, 128], [1, NFT * D]]))

                        def proj_fm(dst_sb, wsb, ev_w, ev_war):
                            ev_e = ev_war
                            ev_mm = None
                            for o in range(8):
                                for c in range(4):
                                    for i in range(8):
                                        ev_mm = em.op("pe", mk(
                                            "matmul",
                                            qkv_ps[:, c * 512:(c + 1) * 512],
                                            wsb[:, i * D + o * 128:
                                                i * D + o * 128 + 128],
                                            AP(y_fm, i * S + c * 512,
                                               [[NFT * S, 128], [1, 512]]),
                                            start=(i == 0), stop=(i == 7)),
                                            deps=[ev_w, ev_yev, ev_e])
                                ev_e = em.op("dve", mk(
                                    "tensor_copy",
                                    out=dst_sb[:, o * S:(o + 1) * S],
                                    in_=qkv_ps[:]), deps=[ev_mm])
                            return ev_mm, ev_e

                        ev_qmm, ev_qe = proj_fm(Q_sb, ws0, ev_wq, None)
                        ev_kmm, ev_ke = proj_fm(K_sb, ws1, ev_wk, ev_qe)
                        ev_wv = em.dma("pl", ws0[:],
                                       AP(wv_d, 0, [[NFT * D, 128], [1, NFT * D]]),
                                       deps=[ev_qmm])
                        ev_vones = em.op("dve", mk(
                            "memset",
                            AP(V_sb, 64, [[NKT * H * 65, 128], [H * 65, NKT],
                                          [65, H], [1, 1]]), 1.0))
                    em.gate()

                    with PS("v_ps", [128, 1024], F32) as v_ps:
                        ev_ve = None
                        for t in range(NTT):
                            ev_vmm = None
                            for i in range(8):
                                for c in range(2):
                                    ev_vmm = em.op("pe", mk(
                                        "matmul", v_ps[:, c * 512:(c + 1) * 512],
                                        AP(y_fm, i * S + t * 128,
                                           [[NFT * S, 128], [1, 128]]),
                                        ws0[:, i * D + c * 512:
                                            i * D + (c + 1) * 512],
                                        start=(i == 0), stop=(i == 7)),
                                        deps=[ev_wv, ev_yev, ev_ve])
                            ev_ve = em.op("dve", mk(
                                "tensor_copy",
                                out=AP(V_sb, t * H * 65,
                                       [[NKT * H * 65, 128], [65, H], [1, 64]]),
                                in_=AP(v_ps, 0, [[1024, 128], [64, H], [1, 64]])),
                                deps=[ev_vmm])
                    em.gate()

                    with PS("n_ps", [16, 512], F32) as n_ps:
                        ev_sq = None
                        ev_red = None
                        ev_nmm = None
                        for which, src, ev_src in ((0, Q_sb, ev_qe),
                                                   (1, K_sb, ev_ke)):
                            for c in range(4):
                                for ft in range(NFT):
                                    ev_sq = em.op("act", mk(
                                        "square", out=nsq[:],
                                        in_=src[:, ft * S + c * 512:
                                                ft * S + (c + 1) * 512]),
                                        deps=[ev_src, ev_nmm, ev_sq])
                                    ev_nmm = em.op("pe", mk(
                                        "matmul", n_ps[:16, :],
                                        ind_bf[:, ft * H:(ft + 1) * H], nsq[:],
                                        start=(ft == 0), stop=(ft == 7)),
                                        deps=[ev_sq, ev_ind, ev_red])
                                ev_red = em.op("dve", mk(
                                    "reduce_max",
                                    out=nrm_sb[:, 4 * which + c:
                                               4 * which + c + 1],
                                    in_=n_ps[:16, :], axis=AX.X),
                                    deps=[ev_nmm])
                        ev_fn1 = em.fence("dve")
                        ev_q2 = em.op("dve", mk("reduce_max",
                                                out=nrm_sb[:, 8:9],
                                                in_=nrm_sb[:, 0:4], axis=AX.X),
                                      deps=[ev_fn1])
                        ev_k2 = em.op("dve", mk("reduce_max",
                                                out=nrm_sb[:, 9:10],
                                                in_=nrm_sb[:, 4:8], axis=AX.X),
                                      deps=[ev_q2])
                        ev_fn2 = em.fence("dve")
                        e1 = em.op("act", mk("activation", out=la_sb[:],
                                             in_=nrm_sb[:, 8:9], func=AF.Ln),
                                   deps=[ev_fn2])
                        e2 = em.op("act", mk("activation", out=lb_sb[:],
                                             in_=nrm_sb[:, 9:10], func=AF.Ln),
                                   deps=[e1])
                        e3 = em.op("act", mk("add", out=la_sb[:], in_=la_sb[:],
                                             add=lb_sb[:]), deps=[e2])
                        e4 = em.op("act", mk("activation", out=Mcol[:],
                                             in_=la_sb[:], func=AF.Exp,
                                             scale=0.5), deps=[e3])
                        e5 = em.op("act", mk("add", out=Mcol[:], in_=Mcol[:],
                                             add=cmax_sb[:]),
                                   deps=[e4, ev_cmf])
                        e6 = em.op("act", mk("mul", out=b0col[:], in_=Mcol[:],
                                             mul=-1.0), deps=[e5])
                        e7 = em.op("act", mk("activation", out=bPcol[:],
                                             in_=cprc[:], func=AF.Identity,
                                             bias=b0col[:]),
                                   deps=[ev_cp, e6])
                        e8 = em.op("act", mk("activation", out=bFcol[:],
                                             in_=cfrc[:], func=AF.Identity,
                                             bias=b0col[:]),
                                   deps=[ev_cf, e7])
                        ev_fa = em.fence("act")
                        g1 = em.dma("sp", brow_sb[0:1, 0:H], b0col[:],
                                    deps=[ev_fa], slow=True)
                        g2 = em.dma("sp", brow_sb[0:1, H:2 * H], bPcol[:],
                                    deps=[ev_fa], slow=True)
                        g3 = em.dma("sp", brow_sb[0:1, 2 * H:3 * H], bFcol[:],
                                    deps=[ev_fa], slow=True)
                        g4 = em.dma("sp", brow_sb[0:1, 3 * H:4 * H], cprc[:],
                                    deps=[ev_cp], slow=True)
                        g5 = em.dma("sp", brow_sb[0:1, 4 * H:5 * H], cfrc[:],
                                    deps=[ev_cf], slow=True)
                    with PS("b_ps", [128, 5 * H], F32) as b_ps:
                        ev_bmm = em.op("pe", mk("matmul", b_ps[:], ones128[:],
                                                brow_sb[:], start=True,
                                                stop=True),
                                       deps=[g1, g2, g3, g4, g5, ev_ones])
                        c1 = em.op("dve", mk("tensor_copy", out=bias0[:],
                                             in_=b_ps[:, 0:H]), deps=[ev_bmm])
                        c2 = em.op("dve", mk("tensor_copy", out=biasP[:],
                                             in_=b_ps[:, H:2 * H]), deps=[c1])
                        c3a = em.op("dve", mk("tensor_copy", out=biasF[:],
                                              in_=b_ps[:, 2 * H:3 * H]),
                                    deps=[c2])
                        c3b = em.op("dve", mk("tensor_copy", out=biasPr[:],
                                              in_=b_ps[:, 3 * H:4 * H]),
                                    deps=[c3a])
                        c3 = em.op("dve", mk("tensor_copy", out=biasFr[:],
                                             in_=b_ps[:, 4 * H:5 * H]),
                                   deps=[c3b])
                em.gate()

                with ExitStack() as cs:
                    wo_sb = cs.enter_context(SB("wo_sb", [128, NFT * D], BF16))
                    PT_sb = cs.enter_context(SB("PT_sb", [128, 8 * QTB], BF16))
                    shif = [cs.enter_context(SB(f"shif{i}", [128, SH_W], F32))
                            for i in range(4)]
                    ctx_sb = cs.enter_context(SB("ctx_sb", [128, 8 * QTB], BF16))
                    r_row = cs.enter_context(SB("r_row", [1, QTB], F32))
                    bc_sb = cs.enter_context(SB("bc_sb", [64, QTB], F32))
                    xres = [cs.enter_context(SB(f"xres{i}", [128, D], F32))
                            for i in range(2)]
                    ob = [cs.enter_context(SB(f"ob{i}", [128, D], F32))
                          for i in range(2)]

                    ev_wo = em.dma("pl", wo_sb[:],
                                   AP(wo_d, 0, [[NFT * D, 128], [1, NFT * D]]))

                    ev_shif_rd = [None] * 4
                    ev_shif_ld = [None] * 4
                    ev_ctx_used = {0: None, 1: None}
                    ev_wo_done = None
                    ev_bc_copied = None
                    ev_oadd = [None, None]
                    ev_store = [None, None]
                    prev_cmms = [[None] * NKT, [None] * NKT]
                    it = 0
                    for qtb in range(NQTB):
                        qb = qtb * QTB
                        with PS("spA0", [128, QTB], F32) as spA0, \
                             PS("spA1", [128, QTB], F32) as spA1, \
                             PS("spB0", [128, QTB], F32) as spB0, \
                             PS("spB1", [128, QTB], F32) as spB1, \
                             PS("ctxA", [65, QTB], F32) as ctxA, \
                             PS("ctxB", [65, QTB], F32) as ctxB, \
                             PS("bc_ps", [64, QTB], F32) as bc_ps:
                            sps = [[spA0, spA1], [spB0, spB1]]
                            ctxp = [ctxA, ctxB]
                            for g in range(H // 2):
                                fbase = g * S
                                for hh in range(2):
                                    h = 2 * g + hh
                                    sl = (2 * it + hh) % 4
                                    ev_shif_ld[sl] = em.dma(
                                        "sp", shif[sl][:],
                                        AP(bv_d, h * 4095 + SH_OFF,
                                           [[1, 128], [1, SH_W]]),
                                        deps=[ev_shif_rd[sl], ev_bvd])
                                ev_exp = [[None] * NKT, [None] * NKT]
                                ev_cmms = [[None] * NKT, [None] * NKT]
                                ev_add = None

                                def ctx_mm(hh, kt):
                                    h = 2 * g + hh
                                    return em.op("pe", mk(
                                        "matmul", ctxp[hh][:65, :],
                                        AP(V_sb, kt * H * 65 + h * 65,
                                           [[NKT * H * 65, 128], [1, 65]]),
                                        PT_sb[:, ((2 * kt + hh) % 8) * QTB:
                                              ((2 * kt + hh) % 8 + 1) * QTB],
                                        start=(kt == 0),
                                        stop=(kt == NKT - 1)),
                                        deps=[ev_exp[hh][kt], ev_vones,
                                              ev_ctx_used[hh]])

                                for kt in range(NKT):
                                    ssl = kt % 2
                                    tval = qb - kt * 128
                                    band = -639 < tval < 255
                                    for hh in range(2):
                                        h = 2 * g + hh
                                        hp = 64 * hh
                                        sl = (2 * it + hh) % 4
                                        war = (ev_exp[hh][kt - 2]
                                               if kt >= 2 else None)
                                        ev_smm = em.op("pe", mk(
                                            "matmul", sps[hh][ssl][:],
                                            K_sb[hp:hp + 64,
                                                 fbase + kt * 128:
                                                 fbase + kt * 128 + 128],
                                            Q_sb[hp:hp + 64,
                                                 fbase + qb:fbase + qb + QTB],
                                            start=True, stop=True),
                                            deps=[war])
                                        dep_exp = ev_smm
                                        if band:
                                            ev_add = em.op("dve", mk(
                                                "tensor_tensor",
                                                out=sps[hh][ssl][:],
                                                in0=sps[hh][ssl][:],
                                                in1=AP(shif[sl], 639 - tval,
                                                       [[SH_W, 128],
                                                        [-1, QTB]]),
                                                op=ALU.add),
                                                deps=[ev_smm, ev_shif_ld[sl],
                                                      ev_add])
                                            ev_shif_rd[sl] = ev_add
                                            dep_exp = ev_add
                                            bias_col = bias0
                                        elif tval >= 255:
                                            bias_col = biasP
                                        else:
                                            bias_col = biasF
                                        pt_slot = (2 * kt + hh) % 8
                                        pt_war = (ev_cmms[hh][kt - 4]
                                                  if kt >= 4
                                                  else prev_cmms[hh][kt + 12])
                                        ev_exp[hh][kt] = em.op("act", mk(
                                            "activation",
                                            out=PT_sb[:, pt_slot * QTB:
                                                      (pt_slot + 1) * QTB],
                                            in_=sps[hh][ssl][:], func=AF.Exp,
                                            bias=bias_col[:, h:h + 1]),
                                            deps=[dep_exp, pt_war, c3])
                                    if kt >= 1:
                                        for hh in range(2):
                                            ev_cmms[hh][kt - 1] = ctx_mm(hh, kt - 1)
                                for hh in range(2):
                                    ev_cmms[hh][NKT - 1] = ctx_mm(hh, NKT - 1)
                                prev_cmms = ev_cmms
                                for hh in range(2):
                                    hp = 64 * hh
                                    ev_rc = em.op("dve", mk(
                                        "reciprocal", out=r_row[:],
                                        in_=ctxp[hh][64:65, :]),
                                        deps=[ev_cmms[hh][NKT - 1],
                                              ev_bc_copied])
                                    ev_rcf = em.fence("dve")
                                    ev_bcm = em.op("pe", mk(
                                        "matmul", bc_ps[:64, :], ones64[:],
                                        r_row[:], start=True, stop=True),
                                        deps=[ev_rcf, ev_bc_copied, ev_ones])
                                    ev_bc_copied = em.op("dve", mk(
                                        "tensor_copy", out=bc_sb[:],
                                        in_=bc_ps[:64, :]), deps=[ev_bcm])
                                    ev_ctx_used[hh] = em.op("dve", mk(
                                        "tensor_mul",
                                        ctx_sb[hp:hp + 64,
                                               g * QTB:(g + 1) * QTB],
                                        ctxp[hh][0:64, :], bc_sb[:]),
                                        deps=[ev_wo_done, ev_bc_copied])
                                it += 1
                            ev_allctx = ev_ctx_used[1]
                        em.gate()
                        with PS("wop0", [128, D], F32) as wop0, \
                             PS("wop1", [128, D], F32) as wop1:
                            wop = [wop0, wop1]
                            for q4 in range(QTB // 128):
                                qt = qtb * 4 + q4
                                wsl = qt % 2
                                ev_x = em.dma("sp", xres[wsl][:], x_tiles[qt],
                                              deps=[ev_oadd[wsl]])
                                ev_wmm = None
                                for g in range(H // 2):
                                    for c in range(2):
                                        ev_wmm = em.op("pe", mk(
                                            "matmul",
                                            wop[wsl][:, c * 512:(c + 1) * 512],
                                            ctx_sb[:, g * QTB + q4 * 128:
                                                   g * QTB + q4 * 128 + 128],
                                            wo_sb[:, g * D + c * 512:
                                                  g * D + (c + 1) * 512],
                                            start=(g == 0),
                                            stop=(g == H // 2 - 1)),
                                            deps=[ev_allctx, ev_wo,
                                                  ev_oadd[wsl]])
                                ev_wo_done = ev_wmm
                                ev_oadd[wsl] = em.op("dve", mk(
                                    "tensor_add", ob[wsl][:], wop[wsl][:],
                                    xres[wsl][:]),
                                    deps=[ev_wmm, ev_x, ev_store[wsl]])
                                ev_store[wsl] = em.dma("sp", out_tiles[qt],
                                                       ob[wsl][:],
                                                       deps=[ev_oadd[wsl]])
                        em.gate()
            em.gate()

            with ExitStack() as fs:
                a_sb = fs.enter_context(SB("a_sb", [128, 32 * CH], BF16))
                for ch in range(NCH):
                    nt2 = CH // 128
                    base_t = ch * nt2
                    ev_ln_loads = []
                    with PS("tp2_ps", [128, 1024], BF16) as tp2_ps, \
                         PS("hp0", [128, CH], F32) as hp0, \
                         PS("hp1", [128, CH], F32) as hp1:
                        hps = [hp0, hp1]
                        with ExitStack() as ys:
                            y2_fm = ys.enter_context(
                                SB("y2_fm", [128, NFT * CH], BF16))
                            xt2 = [ys.enter_context(
                                SB(f"xt2{i}", [128, D], F32)) for i in range(2)]
                            x22 = ys.enter_context(SB("x22", [128, D], F32))
                            y2_tm = ys.enter_context(SB("y2_tm", [128, D], BF16))
                            l2 = [ys.enter_context(SB(f"l2{i}", [128, nt2], F32))
                                  for i in range(8)]
                            wcol = [ys.enter_context(
                                SB(f"wcol{i}", [128, CH], BF16))
                                for i in range(4)]
                            ftmp = [ys.enter_context(
                                SB(f"ftmp{i}", [128, CH], F32))
                                for i in range(2)]

                            ev_y2 = emit_ln(
                                [out_tiles[base_t + t] for t in range(nt2)],
                                nt2,
                                (xt2, x22, l2[0], l2[1], l2[2], l2[3], l2[4],
                                 l2[5], l2[6], l2[7], y2_tm, tp2_ps),
                                y2_fm, CH)

                            def updoor(w_dram, scale_sb, ev_scale, func,
                                       do_mult):
                                ev_wc = [None] * 4
                                ev_ev = {}
                                ev_mul = [None, None]
                                for o in range(32):
                                    slw = o % 4
                                    ev_wc[slw] = em.dma(
                                        "pl", wcol[slw][:],
                                        AP(w_dram, o * 1024,
                                           [[32768, 128], [1, 1024]]),
                                        deps=[ev_ev.get(o - 4)])
                                    pslw = o % 2
                                    ev_mm = None
                                    for i in range(8):
                                        for c in range(2):
                                            ev_mm = em.op("pe", mk(
                                                "matmul",
                                                hps[pslw][:, c * 512:
                                                          (c + 1) * 512],
                                                wcol[slw][:, i * 128:
                                                          (i + 1) * 128],
                                                y2_fm[:, i * CH + c * 512:
                                                      i * CH + (c + 1) * 512],
                                                start=(i == 0), stop=(i == 7)),
                                                deps=[ev_wc[slw], ev_y2,
                                                      ev_ev.get(o - 2)])
                                    if do_mult:
                                        ev_sl = em.op("act", mk(
                                            "activation", out=ftmp[pslw][:],
                                            in_=hps[pslw][:], func=func,
                                            scale=scale_sb[:, o:o + 1]),
                                            deps=[ev_mm, ev_mul[pslw],
                                                  ev_scale])
                                        ev_mul[pslw] = em.op("dve", mk(
                                            "tensor_mul",
                                            a_sb[:, o * CH:(o + 1) * CH],
                                            ftmp[pslw][:],
                                            a_sb[:, o * CH:(o + 1) * CH]),
                                            deps=[ev_sl])
                                        ev_ev[o] = ev_sl
                                    else:
                                        ev_ev[o] = em.op("act", mk(
                                            "activation",
                                            out=a_sb[:, o * CH:(o + 1) * CH],
                                            in_=hps[pslw][:], func=func,
                                            scale=scale_sb[:, o:o + 1]),
                                            deps=[ev_mm, ev_scale])
                                return ev_mul if do_mult else ev_ev

                            updoor(w2_d, s2_sb, ev_s2, AF.Copy, False)
                            em.gate()
                            updoor(w1_d, s1_sb, ev_s1, AF.Silu, True)
                            em.gate()

                        with ExitStack() as bs:
                            b_sb = bs.enter_context(
                                SB("b_sb", [128, 32 * CH], BF16))
                            hcol = [bs.enter_context(
                                SB(f"hcol{i}", [128, 4096], BF16))
                                for i in range(3)]

                            def hid(w_dram, scale_sb, ev_scale, src, dst):
                                ev_hc = [None] * 3
                                ev_ev = {}
                                for o in range(32):
                                    slw = o % 3
                                    pslw = o % 2
                                    ev_hc[slw] = em.dma(
                                        "pl", hcol[slw][:],
                                        AP(w_dram, o * 4096,
                                           [[131072, 128], [1, 4096]]),
                                        deps=[ev_ev.get(o - 3)])
                                    ev_mm = None
                                    for i in range(32):
                                        for c in range(2):
                                            ev_mm = em.op("pe", mk(
                                                "matmul",
                                                hps[pslw][:, c * 512:
                                                          (c + 1) * 512],
                                                hcol[slw][:, i * 128:
                                                          (i + 1) * 128],
                                                src[:, i * CH + c * 512:
                                                    i * CH + (c + 1) * 512],
                                                start=(i == 0), stop=(i == 31)),
                                                deps=[ev_hc[slw],
                                                      ev_ev.get(o - 2)])
                                    ev_ev[o] = em.op("act", mk(
                                        "activation",
                                        out=dst[:, o * CH:(o + 1) * CH],
                                        in_=hps[pslw][:], func=AF.Gelu,
                                        scale=scale_sb[:, o:o + 1]),
                                        deps=[ev_mm, ev_scale])

                            hid(h1_d, sh1_sb, ev_sh1, a_sb, b_sb)
                            em.gate()
                            hid(h2_d, sh2_sb, ev_sh2, b_sb, a_sb)
                            em.gate()

                        with SB("w3h", [128, 32 * 512], BF16) as w3h, \
                             SB("otb0", [128, 512], F32) as otb0, \
                             SB("otb1", [128, 512], F32) as otb1, \
                             PS("w3p0", [128, 512], F32) as w3p0, \
                             PS("w3p1", [128, 512], F32) as w3p1:
                            otb = [otb0, otb1]
                            w3p = [w3p0, w3p1]
                            ev_w3mm_prev = None
                            ev_ot = [None, None]
                            ev_st2 = [None, None]
                            for half in range(2):
                                ev_w3l = em.dma(
                                    "pl", w3h[:],
                                    AP(w3_d, half * 32 * 512,
                                       [[2 * 32 * 512, 128], [1, 32 * 512]]),
                                    deps=[ev_w3mm_prev])
                                for t in range(nt2):
                                    slw = t % 2
                                    row0 = (base_t + t) * 128
                                    xp_ap = AP(out_d,
                                               sq * S * D + row0 * D
                                               + half * 512,
                                               [[D, 128], [1, 512]])
                                    ev_mm = None
                                    for i in range(32):
                                        ev_mm = em.op("pe", mk(
                                            "matmul", w3p[slw][:],
                                            a_sb[:, i * CH + t * 128:
                                                 i * CH + t * 128 + 128],
                                            w3h[:, i * 512:(i + 1) * 512],
                                            start=(i == 0), stop=(i == 31)),
                                            deps=[ev_w3l, ev_ot[slw]])
                                    ev_w3mm_prev = ev_mm
                                    ev_ot[slw] = em.op("dve", mk(
                                        "tensor_mul", otb[slw][:],
                                        w3p[slw][:],
                                        s3b_sb[:, half * 512:
                                               (half + 1) * 512]),
                                        deps=[ev_mm, ev_st2[slw]])
                                    ev_st2[slw] = em.dma(
                                        "pl", xp_ap, otb[slw][:],
                                        deps=[ev_ot[slw]], accum=True)
                                    final_store_evs.append(ev_st2[slw])
                    em.gate()

        bodies = em.finalize(sems_c, sems_d, final_waits=final_store_evs[-8:])

        with nc.Block() as blk:
            blk.tensor(bodies["pe"])
            blk.scalar(bodies["act"])
            blk.vector(bodies["dve"])
            blk.sync(bodies["sp"])
            blk.gpsimd(bodies["pl"])

    return nc


def _host_prep(inputs):
    x = np.ascontiguousarray(np.asarray(inputs["L0_attn_x"], dtype=np.float32))
    wq = np.asarray(inputs["L0_attn_wq"], np.float32)
    wk = np.asarray(inputs["L0_attn_wk"], np.float32)
    wv = np.asarray(inputs["L0_attn_wv"], np.float32)
    wo = np.asarray(inputs["L0_attn_wo"], np.float32)
    rb = np.ascontiguousarray(np.asarray(inputs["L0_attn_rel_bias"], np.float32))
    w1 = np.asarray(inputs["L0_ffn_w1_w"], np.float32)
    w2 = np.asarray(inputs["L0_ffn_w2_w"], np.float32)
    h1 = np.asarray(inputs["L0_ffn_h1_w"], np.float32)
    h2 = np.asarray(inputs["L0_ffn_h2_w"], np.float32)
    w3 = np.asarray(inputs["L0_ffn_w3_w"], np.float32)
    s1 = np.asarray(inputs["L0_ffn_w1_s"], np.float32)
    s2 = np.asarray(inputs["L0_ffn_w2_s"], np.float32)
    sh1 = np.asarray(inputs["L0_ffn_h1_s"], np.float32)
    sh2 = np.asarray(inputs["L0_ffn_h2_s"], np.float32)
    s3 = np.asarray(inputs["L0_ffn_w3_s"], np.float32)

    def qkvr(w):
        return np.ascontiguousarray(w.reshape(NFT, 128, D).transpose(1, 0, 2))

    wo_r = np.ascontiguousarray(
        wo.reshape(NFT, 128, D).transpose(1, 0, 2).reshape(128, NFT * D))

    def tile_io(wT, ni, no):
        return np.ascontiguousarray(
            wT.reshape(ni, 128, no, 128).transpose(1, 2, 0, 3))

    w1tl = tile_io(w1.T, 8, 32)
    w2tl = tile_io(w2.T, 8, 32)
    h1tl = tile_io(h1.T, 32, 32)
    h2tl = tile_io(h2.T, 32, 32)
    w3r = np.ascontiguousarray(
        w3.T.reshape(32, 128, 2, 512).transpose(1, 2, 0, 3))

    s1r = np.ascontiguousarray(s1.reshape(32, 128).T)
    s2r = np.ascontiguousarray(s2.reshape(32, 128).T)
    sh1r = np.ascontiguousarray(sh1.reshape(32, 128).T)
    sh2r = np.ascontiguousarray(sh2.reshape(32, 128).T)
    s3row = np.ascontiguousarray(s3.reshape(1, D))

    d = np.arange(-2047, 2048)
    n = -d
    ret = (n < 0).astype(np.int32) * 16
    na = np.abs(n)
    is_small = na < 8
    val_if_large = 8 + (np.log(np.maximum(na, 1).astype(np.float32) / 8.0)
                        / math.log(128.0 / 8.0) * 8.0).astype(np.int32)
    val_if_large = np.minimum(val_if_large, 15)
    bucket = ret + np.where(is_small, na, val_if_large)
    onehot = (bucket[None, :] == np.arange(32)[:, None]).astype(np.float32)

    ident = np.eye(128, dtype=np.float32)
    p = np.arange(128)
    indall = np.zeros((128, NFT * H), np.float32)
    for ft in range(NFT):
        indall[p < 64, ft * H + 2 * ft] = 1.0
        indall[p >= 64, ft * H + 2 * ft + 1] = 1.0

    shared = dict(wqr=qkvr(wq), wkr=qkvr(wk), wvr=qkvr(wv), wo_r=wo_r,
                  w1tl=w1tl, w2tl=w2tl, h1tl=h1tl, h2tl=h2tl, w3r=w3r,
                  s1r=s1r, s2r=s2r, sh1r=sh1r, sh2r=sh2r, s3row=s3row,
                  rb=rb, onehot=onehot, ident=ident, indall=indall)
    in_maps = []
    for c in range(8):
        m = dict(shared)
        m["x"] = np.ascontiguousarray(x[2 * c:2 * c + 2])
        in_maps.append(m)
    return in_maps


def kernel(**inputs):
    in_maps = _host_prep(inputs)
    nc = build_program()
    res = run_bass_kernel_spmd(nc, in_maps, core_ids=list(range(8)))
    return np.concatenate([r["out"] for r in res.results],
                          axis=0).reshape(16, S, D)


# version 07 (final)
# speedup vs baseline: 1.0708x; 1.0708x over previous
import math
from collections import defaultdict
import numpy as np
import concourse.bass as bass
import concourse.mybir as mybir
from concourse.ap import AP
from concourse.bass_utils import run_bass_kernel_spmd

F32 = mybir.dt.float32
BF16 = mybir.dt.bfloat16
AF = mybir.ActivationFunctionType
ALU = mybir.AluOpType
AX = mybir.AxisListType

S, D, H, DK, DH = 2048, 1024, 16, 64, 4096
NSEQ = 2
NTT = S // 128
NFT = D // 128
QTB = 512
NQTB = S // QTB
NKT = S // 128
CH = 1024
NCH = S // CH
EPS = 1e-5
SH_W = 1152
SH_OFF = 1408

COMPUTE_ENGINES = ("pe", "act", "dve")
DMA_ENGINES = ("sp", "pl")
ENGINES = COMPUTE_ENGINES + DMA_ENGINES
NSEM_DMA = 6


class Emitter:
    def __init__(self):
        self.prog = {e: [] for e in ENGINES}
        self.ndma = {e: 0 for e in DMA_ENGINES}
        self._pending_gate = {}

    def _take_gate(self, eng):
        return self._pending_gate.pop(eng, [])

    def op(self, eng, fn, deps=()):
        waits = self._take_gate(eng) + [d for d in deps if d is not None]
        self.prog[eng].append({"k": "c", "fn": fn, "waits": waits,
                               "marked": False})
        return ("c", eng, len(self.prog[eng]) - 1)

    def fence(self, eng, deps=()):
        return self.op(eng, lambda e: e.drain(), deps=deps)

    def dma(self, eng, out_ap, in_ap, deps=(), slow=False, accum=False):
        i = self.ndma[eng]
        self.ndma[eng] += 1
        slot, k = i % NSEM_DMA, i // NSEM_DMA

        def fn(e, o=out_ap, i_=in_ap, s=slow, a=accum):
            if a:
                return e.dma_start(out=o, in_=i_, accum_op=ALU.add,
                                   allow_slow_non_contiguous=s)
            return e.dma_start(out=o, in_=i_, allow_slow_non_contiguous=s)

        waits = self._take_gate(eng) + [d for d in deps if d is not None]
        self.prog[eng].append({"k": "d", "fn": fn, "waits": waits,
                               "slot": slot, "seq": k, "marked": False})
        return ("d", eng, slot, k)

    def barrier(self):
        evs = []
        for e in COMPUTE_ENGINES:
            if self.prog[e]:
                evs.append(("c", e, len(self.prog[e]) - 1))
        for e in DMA_ENGINES:
            seen = {}
            for ent in reversed(self.prog[e]):
                if ent["k"] == "d" and ent["slot"] not in seen:
                    seen[ent["slot"]] = ent["seq"]
                    if len(seen) == NSEM_DMA:
                        break
            for slot, seq in seen.items():
                evs.append(("d", e, slot, seq))
        return evs

    def gate(self):
        evs = self.barrier()
        for e in ENGINES:
            for ev in evs:
                if ev[1] != e or (e in DMA_ENGINES and ev[0] == "d"):
                    self._pending_gate.setdefault(e, []).append(ev)
        return evs

    def finalize(self, sems_c, sems_d, final_waits=()):
        for e in ENGINES:
            for ent in self.prog[e]:
                for ev in ent["waits"]:
                    if ev[0] == "c":
                        self.prog[ev[1]][ev[2]]["marked"] = True
        for ev in final_waits:
            if ev[0] == "c":
                self.prog[ev[1]][ev[2]]["marked"] = True
        cum = {}
        for e in COMPUTE_ENGINES:
            c = 0
            cums = []
            for ent in self.prog[e]:
                if ent["marked"]:
                    c += 1
                cums.append(c)
            cum[e] = cums

        def make_body(e):
            def body(eng):
                waited = defaultdict(int)

                def do_wait(ev):
                    if ev[0] == "c":
                        _, pe_, idx = ev
                        v = cum[pe_][idx]
                        if waited[("c", pe_)] < v:
                            waited[("c", pe_)] = v
                            eng.wait_ge(sems_c[pe_], v)
                    else:
                        _, de, slot, k = ev
                        v = 16 * (k + 1)
                        if waited[("d", de, slot)] < v:
                            waited[("d", de, slot)] = v
                            eng.wait_ge(sems_d[de][slot], v)

                for ent in self.prog[e]:
                    for ev in ent["waits"]:
                        do_wait(ev)
                    if ent["k"] == "d":
                        if ent["seq"] > 0:
                            v = 16 * ent["seq"]
                            if waited[("d", e, ent["slot"])] < v:
                                waited[("d", e, ent["slot"])] = v
                                eng.wait_ge(sems_d[e][ent["slot"]], v)
                        ent["fn"](eng).then_inc(sems_d[e][ent["slot"]], 16)
                    else:
                        inst = ent["fn"](eng)
                        if ent["marked"]:
                            inst.then_inc(sems_c[e], 1)
                if e in DMA_ENGINES:
                    for ev in final_waits:
                        do_wait(ev)
            return body
        return {e: make_body(e) for e in ENGINES}


def mk(fname, *args, **kw):
    def fn(e):
        return getattr(e, fname)(*args, **kw)
    return fn


def build_program():
    nc = bass.Bass("TRN2", target_bir_lowering=False)

    x_d = nc.dram_tensor("x", [NSEQ, S, D], F32, kind="ExternalInput")
    wq_d = nc.dram_tensor("wqr", [128, NFT, D], F32, kind="ExternalInput")
    wk_d = nc.dram_tensor("wkr", [128, NFT, D], F32, kind="ExternalInput")
    wv_d = nc.dram_tensor("wvr", [128, NFT, D], F32, kind="ExternalInput")
    wo_d = nc.dram_tensor("wo_r", [128, NFT * D], F32, kind="ExternalInput")
    w1_d = nc.dram_tensor("w1tl", [128, 32, 8, 128], F32, kind="ExternalInput")
    w2_d = nc.dram_tensor("w2tl", [128, 32, 8, 128], F32, kind="ExternalInput")
    h1_d = nc.dram_tensor("h1tl", [128, 32, 32, 128], F32, kind="ExternalInput")
    h2_d = nc.dram_tensor("h2tl", [128, 32, 32, 128], F32, kind="ExternalInput")
    w3_d = nc.dram_tensor("w3r", [128, 2, 32, 512], F32, kind="ExternalInput")
    s1_d = nc.dram_tensor("s1r", [128, 32], F32, kind="ExternalInput")
    s2_d = nc.dram_tensor("s2r", [128, 32], F32, kind="ExternalInput")
    sh1_d = nc.dram_tensor("sh1r", [128, 32], F32, kind="ExternalInput")
    sh2_d = nc.dram_tensor("sh2r", [128, 32], F32, kind="ExternalInput")
    s3_d = nc.dram_tensor("s3row", [1, D], F32, kind="ExternalInput")
    rb_d = nc.dram_tensor("rb", [32, H], F32, kind="ExternalInput")
    oh_d = nc.dram_tensor("onehot", [32, 4095], F32, kind="ExternalInput")
    id_d = nc.dram_tensor("ident", [128, 128], F32, kind="ExternalInput")
    ind_d = nc.dram_tensor("indall", [128, NFT * H], F32, kind="ExternalInput")
    out_d = nc.dram_tensor("out", [NSEQ, S, D], F32, kind="ExternalOutput")
    bv_d = nc.dram_tensor("bv", [H, 4095], F32, kind="Internal")

    em = Emitter()
    from contextlib import ExitStack

    _uid = [0]

    def SB(name, shape, dtype):
        _uid[0] += 1
        return nc.sbuf_tensor(f"{name}_{_uid[0]}", shape, dtype)

    def PS(name, shape, dtype=F32):
        _uid[0] += 1
        return nc.psum_tensor(f"{name}_{_uid[0]}", shape, dtype)

    final_store_evs = []

    with ExitStack() as outer:
        sems_c = {e: outer.enter_context(nc.semaphore("sem_" + e))
                  for e in COMPUTE_ENGINES}
        sems_d = {e: [outer.enter_context(nc.semaphore(f"sem_{e}{i}"))
                      for i in range(NSEM_DMA)] for e in DMA_ENGINES}

        id_bf = outer.enter_context(SB("id_bf", [128, 128], BF16))
        ind_bf = outer.enter_context(SB("ind_bf", [128, NFT * H], BF16))
        s1_sb = outer.enter_context(SB("s1_sb", [128, 32], F32))
        s2_sb = outer.enter_context(SB("s2_sb", [128, 32], F32))
        sh1_sb = outer.enter_context(SB("sh1_sb", [128, 32], F32))
        sh2_sb = outer.enter_context(SB("sh2_sb", [128, 32], F32))
        s3b_sb = outer.enter_context(SB("s3b_sb", [128, D], F32))
        ones128 = outer.enter_context(SB("ones128", [1, 128], F32))
        ones64 = outer.enter_context(SB("ones64", [1, 64], F32))
        eps_col = outer.enter_context(SB("eps_col", [128, 1], F32))
        cprc = outer.enter_context(SB("cprc", [16, 1], F32))
        cfrc = outer.enter_context(SB("cfrc", [16, 1], F32))
        cmax_sb = outer.enter_context(SB("cmax_sb", [16, 1], F32))
        nrm_sb = outer.enter_context(SB("nrm_sb", [16, 12], F32))
        la_sb = outer.enter_context(SB("la_sb", [16, 1], F32))
        lb_sb = outer.enter_context(SB("lb_sb", [16, 1], F32))
        Mcol = outer.enter_context(SB("Mcol", [16, 1], F32))
        b0col = outer.enter_context(SB("b0col", [16, 1], F32))
        bPcol = outer.enter_context(SB("bPcol", [16, 1], F32))
        bFcol = outer.enter_context(SB("bFcol", [16, 1], F32))
        brow_sb = outer.enter_context(SB("brow_sb", [1, 5 * H], F32))
        biasP = outer.enter_context(SB("biasP", [128, H], F32))
        biasF = outer.enter_context(SB("biasF", [128, H], F32))
        bias0 = outer.enter_context(SB("bias0", [128, H], F32))
        biasPr = outer.enter_context(SB("biasPr", [128, H], F32))
        biasFr = outer.enter_context(SB("biasFr", [128, H], F32))

        ev_id = em.dma("pl", id_bf[:], id_d.ap())
        ev_ind = em.dma("pl", ind_bf[:], ind_d.ap())
        ev_s1 = em.dma("sp", s1_sb[:], s1_d.ap())
        ev_s2 = em.dma("sp", s2_sb[:], s2_d.ap())
        ev_sh1 = em.dma("sp", sh1_sb[:], sh1_d.ap())
        ev_sh2 = em.dma("sp", sh2_sb[:], sh2_d.ap())
        em.op("dve", mk("memset", ones128[:], 1.0))
        em.op("dve", mk("memset", ones64[:], 1.0))
        em.op("dve", mk("memset", eps_col[:], EPS))
        ev_ones = em.fence("dve")

        with SB("rb_sb", [32, H], F32) as rb_sb, \
             SB("oh_sb", [32, 4095], F32) as oh_sb, \
             SB("bv_sb", [16, 4096], F32) as bv_sb, \
             SB("s3r_sb", [1, D], F32) as s3r_sb, \
             PS("set_ps", [128, 1024], F32) as set_ps, \
             PS("bvp", [16, 512], F32) as bvp:
            ev_s3 = em.dma("sp", s3r_sb[:], s3_d.ap())
            ev_rb = em.dma("sp", rb_sb[:], rb_d.ap())
            ev_oh = em.dma("sp", oh_sb[:], oh_d.ap())
            ev_mmb = None
            for c in range(2):
                ev_mmb = em.op("pe", mk("matmul", set_ps[:, c * 512:(c + 1) * 512],
                                        ones128[:], s3r_sb[:, c * 512:(c + 1) * 512],
                                        start=True, stop=True),
                               deps=[ev_s3, ev_ones, ev_mmb])
            em.op("dve", mk("tensor_copy", out=s3b_sb[:], in_=set_ps[:]),
                  deps=[ev_mmb, ev_ones])
            ev_bve = None
            for c in range(8):
                n = min(512, 4095 - c * 512)
                ev_bvm = em.op("pe", mk("matmul", bvp[:16, 0:n],
                                        rb_sb[:], oh_sb[:, c * 512:c * 512 + n],
                                        start=True, stop=True),
                               deps=[ev_rb, ev_oh, ev_bve])
                ev_bve = em.op("dve", mk("tensor_copy",
                                         out=bv_sb[:16, c * 512:c * 512 + n],
                                         in_=bvp[:16, 0:n]), deps=[ev_bvm])
            ev_cm0 = em.op("dve", mk("reduce_max", out=cmax_sb[:],
                                     in_=bv_sb[:16, 0:4095], axis=AX.X),
                           deps=[ev_bve])
            ev_cmf = em.fence("dve")
            ev_bvd = em.dma("sp", bv_d.ap(), bv_sb[:16, 0:4095], deps=[ev_bve])
            ev_cp = em.dma("sp", cprc[:], AP(bv_d, 0, [[4095, 16], [1, 1]]),
                           deps=[ev_bvd], slow=True)
            ev_cf = em.dma("sp", cfrc[:], AP(bv_d, 4094, [[4095, 16], [1, 1]]),
                           deps=[ev_bvd], slow=True)
        em.gate()

        x_ap = x_d.ap()
        out_ap = out_d.ap()

        def emit_ln(tiles, nt, bufs, dst_fm, dst_stride):
            (xt, x2, m_sb, v_sb, mean_sb, msq_sb, var_sb, lnv_sb, r_sb,
             mb_sb, y_tm, tp_ps) = bufs
            ev_rd = [None, None]
            ev_x2r = None
            ev_x2w = None
            for t in range(nt):
                sl = t % 2
                ev_x = em.dma("sp", xt[sl][:], tiles[t], deps=ev_rd[sl] or [])
                ev_m = em.op("dve", mk("reduce_sum", out=m_sb[:, t:t + 1],
                                       in_=xt[sl][:], axis=AX.X), deps=[ev_x])
                ev_sq = em.op("act", mk("square", out=x2[:], in_=xt[sl][:]),
                              deps=[ev_x, ev_x2r, ev_x2w])
                ev_x2w = ev_sq
                ev_v = em.op("dve", mk("reduce_sum", out=v_sb[:, t:t + 1],
                                       in_=x2[:], axis=AX.X), deps=[ev_sq])
                ev_rd[sl] = [ev_m, ev_sq]
                ev_x2r = ev_v
            ev_f1 = em.fence("dve")
            ev_mean = em.op("dve", mk("tensor_scalar_mul", mean_sb[:], m_sb[:],
                                      1.0 / D), deps=[ev_f1])
            ev_vv = em.op("dve", mk("tensor_scalar_mul", var_sb[:], v_sb[:],
                                    1.0 / D), deps=[ev_mean])
            ev_msq = em.op("dve", mk("tensor_mul", msq_sb[:], mean_sb[:],
                                     mean_sb[:]), deps=[ev_mean])
            ev_var = em.op("dve", mk("tensor_sub", var_sb[:], var_sb[:],
                                     msq_sb[:]), deps=[ev_vv, ev_msq])
            ev_ln = em.op("act", mk("activation", out=lnv_sb[:], in_=var_sb[:],
                                    func=AF.Ln, bias=eps_col[:]),
                          deps=[ev_var, ev_ones])
            ev_r = em.op("act", mk("activation", out=r_sb[:], in_=lnv_sb[:],
                                   func=AF.Exp, scale=-0.5), deps=[ev_ln])
            ev_mb0 = em.op("dve", mk("tensor_mul", mb_sb[:], mean_sb[:],
                                     r_sb[:]), deps=[ev_r, ev_mean])
            ev_mb = em.op("dve", mk("tensor_scalar_mul", mb_sb[:], mb_sb[:],
                                    -1.0), deps=[ev_mb0])
            ev_app = [None, None]
            ev_tp = None
            ev_yev = None
            for t in range(nt):
                sl = t % 2
                ev_x = em.dma("sp", xt[sl][:], tiles[t],
                              deps=(ev_rd[sl] or []) + [ev_app[sl]])
                ev_rd[sl] = None
                ev_a = em.op("act", mk("activation", out=y_tm[:], in_=xt[sl][:],
                                       func=AF.Identity,
                                       bias=mb_sb[:, t:t + 1],
                                       scale=r_sb[:, t:t + 1]),
                             deps=[ev_x, ev_mb, ev_r, ev_tp])
                ev_app[sl] = ev_a
                for j in range(8):
                    ev_tp = em.op("pe", mk("transpose",
                                           tp_ps[:, j * 128:(j + 1) * 128],
                                           y_tm[:, j * 128:(j + 1) * 128],
                                           id_bf[:]),
                                  deps=[ev_a, ev_id, ev_yev])
                ev_yev = em.op("dve", mk(
                    "tensor_copy",
                    out=AP(dst_fm, t * 128,
                           [[NFT * dst_stride, 128], [dst_stride, 8], [1, 128]]),
                    in_=AP(tp_ps, 0, [[1024, 128], [128, 8], [1, 128]])),
                    deps=[ev_tp])
            return ev_yev

        for sq in range(NSEQ):
            x_tiles = x_ap[sq].rearrange("(t p) d -> t p d", p=128)
            out_tiles = out_ap[sq].rearrange("(t p) d -> t p d", p=128)

            with ExitStack() as attn:
                Q_sb = attn.enter_context(SB("Q_sb", [128, NFT * S], BF16))
                K_sb = attn.enter_context(SB("K_sb", [128, NFT * S], BF16))
                V_sb = attn.enter_context(SB("V_sb", [128, NKT * H * 65], BF16))

                with ExitStack() as ab:
                    y_fm = ab.enter_context(SB("y_fm", [128, NFT * S], BF16))
                    ws0 = ab.enter_context(SB("ws0", [128, NFT * D], BF16))
                    ws1 = ab.enter_context(SB("ws1", [128, NFT * D], BF16))
                    xt = [ab.enter_context(SB(f"xt{i}", [128, D], F32))
                          for i in range(2)]
                    x2 = ab.enter_context(SB("x2", [128, D], F32))
                    y_tm = ab.enter_context(SB("y_tm", [128, D], BF16))
                    lnb = [ab.enter_context(SB(f"lnb{i}", [128, NTT], F32))
                           for i in range(8)]
                    nsq = ab.enter_context(SB("nsq", [128, 512], BF16))

                    with PS("tp_ps", [128, 1024], BF16) as tp_ps, \
                         PS("qkv_ps", [128, S], F32) as qkv_ps:
                        ev_yev = emit_ln(
                            x_tiles, NTT,
                            (xt, x2, lnb[0], lnb[1], lnb[2], lnb[3], lnb[4],
                             lnb[5], lnb[6], lnb[7], y_tm, tp_ps),
                            y_fm, S)

                        ev_wq = em.dma("pl", ws0[:],
                                       AP(wq_d, 0, [[NFT * D, 128], [1, NFT * D]]))
                        ev_wk = em.dma("pl", ws1[:],
                                       AP(wk_d, 0, [[NFT * D, 128], [1, NFT * D]]))

                        def proj_fm(dst_sb, wsb, ev_w, ev_war):
                            ev_ec = list(ev_war) if ev_war else [None] * 4
                            ev_mm = None
                            for o in range(8):
                                for c in range(4):
                                    for i in range(8):
                                        ev_mm = em.op("pe", mk(
                                            "matmul",
                                            qkv_ps[:, c * 512:(c + 1) * 512],
                                            wsb[:, i * D + o * 128:
                                                i * D + o * 128 + 128],
                                            AP(y_fm, i * S + c * 512,
                                               [[NFT * S, 128], [1, 512]]),
                                            start=(i == 0), stop=(i == 7)),
                                            deps=[ev_w, ev_yev, ev_ec[c]])
                                    ev_ec[c] = em.op("dve", mk(
                                        "tensor_copy",
                                        out=dst_sb[:, o * S + c * 512:
                                                   o * S + (c + 1) * 512],
                                        in_=qkv_ps[:, c * 512:(c + 1) * 512]),
                                        deps=[ev_mm])
                            return ev_mm, ev_ec

                        ev_qmm, ev_qe = proj_fm(Q_sb, ws0, ev_wq, None)
                        ev_kmm, ev_ke = proj_fm(K_sb, ws1, ev_wk, ev_qe)
                        ev_qe = ev_qe[-1]
                        ev_ke = ev_ke[-1]
                        ev_wv = em.dma("pl", ws0[:],
                                       AP(wv_d, 0, [[NFT * D, 128], [1, NFT * D]]),
                                       deps=[ev_qmm])
                        ev_vones = em.op("dve", mk(
                            "memset",
                            AP(V_sb, 64, [[NKT * H * 65, 128], [H * 65, NKT],
                                          [65, H], [1, 1]]), 1.0))
                    em.gate()

                    with PS("v_ps", [128, 1024], F32) as v_ps:
                        ev_ve = None
                        for t in range(NTT):
                            ev_vmm = None
                            for i in range(8):
                                for c in range(2):
                                    ev_vmm = em.op("pe", mk(
                                        "matmul", v_ps[:, c * 512:(c + 1) * 512],
                                        AP(y_fm, i * S + t * 128,
                                           [[NFT * S, 128], [1, 128]]),
                                        ws0[:, i * D + c * 512:
                                            i * D + (c + 1) * 512],
                                        start=(i == 0), stop=(i == 7)),
                                        deps=[ev_wv, ev_yev, ev_ve])
                            ev_ve = em.op("dve", mk(
                                "tensor_copy",
                                out=AP(V_sb, t * H * 65,
                                       [[NKT * H * 65, 128], [65, H], [1, 64]]),
                                in_=AP(v_ps, 0, [[1024, 128], [64, H], [1, 64]])),
                                deps=[ev_vmm])
                    em.gate()

                    with PS("n_ps", [16, 512], F32) as n_ps:
                        ev_sq = None
                        ev_red = None
                        ev_nmm = None
                        for which, src, ev_src in ((0, Q_sb, ev_qe),
                                                   (1, K_sb, ev_ke)):
                            for c in range(4):
                                for ft in range(NFT):
                                    ev_sq = em.op("act", mk(
                                        "square", out=nsq[:],
                                        in_=src[:, ft * S + c * 512:
                                                ft * S + (c + 1) * 512]),
                                        deps=[ev_src, ev_nmm, ev_sq])
                                    ev_nmm = em.op("pe", mk(
                                        "matmul", n_ps[:16, :],
                                        ind_bf[:, ft * H:(ft + 1) * H], nsq[:],
                                        start=(ft == 0), stop=(ft == 7)),
                                        deps=[ev_sq, ev_ind, ev_red])
                                ev_red = em.op("dve", mk(
                                    "reduce_max",
                                    out=nrm_sb[:, 4 * which + c:
                                               4 * which + c + 1],
                                    in_=n_ps[:16, :], axis=AX.X),
                                    deps=[ev_nmm])
                        ev_fn1 = em.fence("dve")
                        ev_q2 = em.op("dve", mk("reduce_max",
                                                out=nrm_sb[:, 8:9],
                                                in_=nrm_sb[:, 0:4], axis=AX.X),
                                      deps=[ev_fn1])
                        ev_k2 = em.op("dve", mk("reduce_max",
                                                out=nrm_sb[:, 9:10],
                                                in_=nrm_sb[:, 4:8], axis=AX.X),
                                      deps=[ev_q2])
                        ev_fn2 = em.fence("dve")
                        e1 = em.op("act", mk("activation", out=la_sb[:],
                                             in_=nrm_sb[:, 8:9], func=AF.Ln),
                                   deps=[ev_fn2])
                        e2 = em.op("act", mk("activation", out=lb_sb[:],
                                             in_=nrm_sb[:, 9:10], func=AF.Ln),
                                   deps=[e1])
                        e3 = em.op("act", mk("add", out=la_sb[:], in_=la_sb[:],
                                             add=lb_sb[:]), deps=[e2])
                        e4 = em.op("act", mk("activation", out=Mcol[:],
                                             in_=la_sb[:], func=AF.Exp,
                                             scale=0.5), deps=[e3])
                        e5 = em.op("act", mk("add", out=Mcol[:], in_=Mcol[:],
                                             add=cmax_sb[:]),
                                   deps=[e4, ev_cmf])
                        e6 = em.op("act", mk("mul", out=b0col[:], in_=Mcol[:],
                                             mul=-1.0), deps=[e5])
                        e7 = em.op("act", mk("activation", out=bPcol[:],
                                             in_=cprc[:], func=AF.Identity,
                                             bias=b0col[:]),
                                   deps=[ev_cp, e6])
                        e8 = em.op("act", mk("activation", out=bFcol[:],
                                             in_=cfrc[:], func=AF.Identity,
                                             bias=b0col[:]),
                                   deps=[ev_cf, e7])
                        ev_fa = em.fence("act")
                        g1 = em.dma("sp", brow_sb[0:1, 0:H], b0col[:],
                                    deps=[ev_fa], slow=True)
                        g2 = em.dma("sp", brow_sb[0:1, H:2 * H], bPcol[:],
                                    deps=[ev_fa], slow=True)
                        g3 = em.dma("sp", brow_sb[0:1, 2 * H:3 * H], bFcol[:],
                                    deps=[ev_fa], slow=True)
                        g4 = em.dma("sp", brow_sb[0:1, 3 * H:4 * H], cprc[:],
                                    deps=[ev_cp], slow=True)
                        g5 = em.dma("sp", brow_sb[0:1, 4 * H:5 * H], cfrc[:],
                                    deps=[ev_cf], slow=True)
                    with PS("b_ps", [128, 5 * H], F32) as b_ps:
                        ev_bmm = em.op("pe", mk("matmul", b_ps[:], ones128[:],
                                                brow_sb[:], start=True,
                                                stop=True),
                                       deps=[g1, g2, g3, g4, g5, ev_ones])
                        c1 = em.op("dve", mk("tensor_copy", out=bias0[:],
                                             in_=b_ps[:, 0:H]), deps=[ev_bmm])
                        c2 = em.op("dve", mk("tensor_copy", out=biasP[:],
                                             in_=b_ps[:, H:2 * H]), deps=[c1])
                        c3a = em.op("dve", mk("tensor_copy", out=biasF[:],
                                              in_=b_ps[:, 2 * H:3 * H]),
                                    deps=[c2])
                        c3b = em.op("dve", mk("tensor_copy", out=biasPr[:],
                                              in_=b_ps[:, 3 * H:4 * H]),
                                    deps=[c3a])
                        c3 = em.op("dve", mk("tensor_copy", out=biasFr[:],
                                             in_=b_ps[:, 4 * H:5 * H]),
                                   deps=[c3b])
                em.gate()

                with ExitStack() as cs:
                    wo_sb = cs.enter_context(SB("wo_sb", [128, NFT * D], BF16))
                    PT_sb = cs.enter_context(SB("PT_sb", [128, 8 * QTB], BF16))
                    shif = [cs.enter_context(SB(f"shif{i}", [128, SH_W], F32))
                            for i in range(4)]
                    ctx_sb = cs.enter_context(SB("ctx_sb", [128, 8 * QTB], BF16))
                    r_row = cs.enter_context(SB("r_row", [1, QTB], F32))
                    bc_sb = cs.enter_context(SB("bc_sb", [64, QTB], F32))
                    xres = [cs.enter_context(SB(f"xres{i}", [128, D], F32))
                            for i in range(2)]
                    ob = [cs.enter_context(SB(f"ob{i}", [128, D], F32))
                          for i in range(2)]

                    ev_wo = em.dma("pl", wo_sb[:],
                                   AP(wo_d, 0, [[NFT * D, 128], [1, NFT * D]]))

                    ev_shif_rd = [None] * 4
                    ev_shif_ld = [None] * 4
                    ev_ctx_used = {0: None, 1: None}
                    ev_wo_done = None
                    ev_bc_copied = None
                    ev_oadd = [None, None]
                    ev_store = [None, None]
                    prev_cmms = [[None] * NKT, [None] * NKT]
                    it = 0
                    for qtb in range(NQTB):
                        qb = qtb * QTB
                        with PS("spA0", [128, QTB], F32) as spA0, \
                             PS("spA1", [128, QTB], F32) as spA1, \
                             PS("spB0", [128, QTB], F32) as spB0, \
                             PS("spB1", [128, QTB], F32) as spB1, \
                             PS("ctxA", [65, QTB], F32) as ctxA, \
                             PS("ctxB", [65, QTB], F32) as ctxB, \
                             PS("bc_ps", [64, QTB], F32) as bc_ps:
                            sps = [[spA0, spA1], [spB0, spB1]]
                            ctxp = [ctxA, ctxB]
                            for g in range(H // 2):
                                fbase = g * S
                                for hh in range(2):
                                    h = 2 * g + hh
                                    sl = (2 * it + hh) % 4
                                    ev_shif_ld[sl] = em.dma(
                                        "sp", shif[sl][:],
                                        AP(bv_d, h * 4095 + SH_OFF,
                                           [[1, 128], [1, SH_W]]),
                                        deps=[ev_shif_rd[sl], ev_bvd])
                                ev_exp = [[None] * NKT, [None] * NKT]
                                ev_cmms = [[None] * NKT, [None] * NKT]
                                ev_add = None

                                def ctx_mm(hh, kt):
                                    h = 2 * g + hh
                                    return em.op("pe", mk(
                                        "matmul", ctxp[hh][:65, :],
                                        AP(V_sb, kt * H * 65 + h * 65,
                                           [[NKT * H * 65, 128], [1, 65]]),
                                        PT_sb[:, ((2 * kt + hh) % 8) * QTB:
                                              ((2 * kt + hh) % 8 + 1) * QTB],
                                        start=(kt == 0),
                                        stop=(kt == NKT - 1)),
                                        deps=[ev_exp[hh][kt], ev_vones,
                                              ev_ctx_used[hh]])

                                for kt in range(NKT):
                                    ssl = kt % 2
                                    tval = qb - kt * 128
                                    band = -639 < tval < 255
                                    for hh in range(2):
                                        h = 2 * g + hh
                                        hp = 64 * hh
                                        sl = (2 * it + hh) % 4
                                        war = (ev_exp[hh][kt - 2]
                                               if kt >= 2 else None)
                                        ev_smm = em.op("pe", mk(
                                            "matmul", sps[hh][ssl][:],
                                            K_sb[hp:hp + 64,
                                                 fbase + kt * 128:
                                                 fbase + kt * 128 + 128],
                                            Q_sb[hp:hp + 64,
                                                 fbase + qb:fbase + qb + QTB],
                                            start=True, stop=True),
                                            deps=[war])
                                        dep_exp = ev_smm
                                        if band:
                                            ev_add = em.op("dve", mk(
                                                "tensor_tensor",
                                                out=sps[hh][ssl][:],
                                                in0=sps[hh][ssl][:],
                                                in1=AP(shif[sl], 639 - tval,
                                                       [[SH_W, 128],
                                                        [-1, QTB]]),
                                                op=ALU.add),
                                                deps=[ev_smm, ev_shif_ld[sl],
                                                      ev_add])
                                            ev_shif_rd[sl] = ev_add
                                            dep_exp = ev_add
                                            bias_col = bias0
                                        elif tval >= 255:
                                            bias_col = biasP
                                        else:
                                            bias_col = biasF
                                        pt_slot = (2 * kt + hh) % 8
                                        pt_war = (ev_cmms[hh][kt - 4]
                                                  if kt >= 4
                                                  else prev_cmms[hh][kt + 12])
                                        ev_exp[hh][kt] = em.op("act", mk(
                                            "activation",
                                            out=PT_sb[:, pt_slot * QTB:
                                                      (pt_slot + 1) * QTB],
                                            in_=sps[hh][ssl][:], func=AF.Exp,
                                            bias=bias_col[:, h:h + 1]),
                                            deps=[dep_exp, pt_war, c3])
                                    if kt >= 1:
                                        for hh in range(2):
                                            ev_cmms[hh][kt - 1] = ctx_mm(hh, kt - 1)
                                for hh in range(2):
                                    ev_cmms[hh][NKT - 1] = ctx_mm(hh, NKT - 1)
                                prev_cmms = ev_cmms
                                for hh in range(2):
                                    hp = 64 * hh
                                    ev_rc = em.op("dve", mk(
                                        "reciprocal", out=r_row[:],
                                        in_=ctxp[hh][64:65, :]),
                                        deps=[ev_cmms[hh][NKT - 1],
                                              ev_bc_copied])
                                    ev_rcf = em.fence("dve")
                                    ev_bcm = em.op("pe", mk(
                                        "matmul", bc_ps[:64, :], ones64[:],
                                        r_row[:], start=True, stop=True),
                                        deps=[ev_rcf, ev_bc_copied, ev_ones])
                                    ev_bc_copied = em.op("dve", mk(
                                        "tensor_copy", out=bc_sb[:],
                                        in_=bc_ps[:64, :]), deps=[ev_bcm])
                                    ev_ctx_used[hh] = em.op("dve", mk(
                                        "tensor_mul",
                                        ctx_sb[hp:hp + 64,
                                               g * QTB:(g + 1) * QTB],
                                        ctxp[hh][0:64, :], bc_sb[:]),
                                        deps=[ev_wo_done, ev_bc_copied])
                                it += 1
                            ev_allctx = ev_ctx_used[1]
                        em.gate()
                        with PS("wop0", [128, D], F32) as wop0, \
                             PS("wop1", [128, D], F32) as wop1:
                            wop = [wop0, wop1]
                            for q4 in range(QTB // 128):
                                qt = qtb * 4 + q4
                                wsl = qt % 2
                                ev_x = em.dma("sp", xres[wsl][:], x_tiles[qt],
                                              deps=[ev_oadd[wsl]])
                                ev_wmm = None
                                for g in range(H // 2):
                                    for c in range(2):
                                        ev_wmm = em.op("pe", mk(
                                            "matmul",
                                            wop[wsl][:, c * 512:(c + 1) * 512],
                                            ctx_sb[:, g * QTB + q4 * 128:
                                                   g * QTB + q4 * 128 + 128],
                                            wo_sb[:, g * D + c * 512:
                                                  g * D + (c + 1) * 512],
                                            start=(g == 0),
                                            stop=(g == H // 2 - 1)),
                                            deps=[ev_allctx, ev_wo,
                                                  ev_oadd[wsl]])
                                ev_wo_done = ev_wmm
                                ev_oadd[wsl] = em.op("dve", mk(
                                    "tensor_add", ob[wsl][:], wop[wsl][:],
                                    xres[wsl][:]),
                                    deps=[ev_wmm, ev_x, ev_store[wsl]])
                                ev_store[wsl] = em.dma("sp", out_tiles[qt],
                                                       ob[wsl][:],
                                                       deps=[ev_oadd[wsl]])
                        em.gate()
            em.gate()

            with ExitStack() as fs:
                a_sb = fs.enter_context(SB("a_sb", [128, 32 * CH], BF16))
                for ch in range(NCH):
                    nt2 = CH // 128
                    base_t = ch * nt2
                    ev_ln_loads = []
                    with PS("tp2_ps", [128, 1024], BF16) as tp2_ps, \
                         PS("hp0", [128, CH], F32) as hp0, \
                         PS("hp1", [128, CH], F32) as hp1:
                        hps = [hp0, hp1]
                        with ExitStack() as ys:
                            y2_fm = ys.enter_context(
                                SB("y2_fm", [128, NFT * CH], BF16))
                            xt2 = [ys.enter_context(
                                SB(f"xt2{i}", [128, D], F32)) for i in range(2)]
                            x22 = ys.enter_context(SB("x22", [128, D], F32))
                            y2_tm = ys.enter_context(SB("y2_tm", [128, D], BF16))
                            l2 = [ys.enter_context(SB(f"l2{i}", [128, nt2], F32))
                                  for i in range(8)]
                            wcol = [ys.enter_context(
                                SB(f"wcol{i}", [128, CH], BF16))
                                for i in range(4)]
                            ftmp = [ys.enter_context(
                                SB(f"ftmp{i}", [128, CH], F32))
                                for i in range(2)]

                            ev_y2 = emit_ln(
                                [out_tiles[base_t + t] for t in range(nt2)],
                                nt2,
                                (xt2, x22, l2[0], l2[1], l2[2], l2[3], l2[4],
                                 l2[5], l2[6], l2[7], y2_tm, tp2_ps),
                                y2_fm, CH)

                            def updoor(w_dram, scale_sb, ev_scale, func,
                                       do_mult):
                                ev_wc = [None] * 4
                                ev_ev = {}
                                ev_mul = [None, None]
                                for o in range(32):
                                    slw = o % 4
                                    ev_wc[slw] = em.dma(
                                        "pl", wcol[slw][:],
                                        AP(w_dram, o * 1024,
                                           [[32768, 128], [1, 1024]]),
                                        deps=[ev_ev.get(o - 4)])
                                    pslw = o % 2
                                    ev_mm = None
                                    for i in range(8):
                                        for c in range(2):
                                            ev_mm = em.op("pe", mk(
                                                "matmul",
                                                hps[pslw][:, c * 512:
                                                          (c + 1) * 512],
                                                wcol[slw][:, i * 128:
                                                          (i + 1) * 128],
                                                y2_fm[:, i * CH + c * 512:
                                                      i * CH + (c + 1) * 512],
                                                start=(i == 0), stop=(i == 7)),
                                                deps=[ev_wc[slw], ev_y2,
                                                      ev_ev.get(o - 2)])
                                    if do_mult:
                                        ev_sl = em.op("act", mk(
                                            "activation", out=ftmp[pslw][:],
                                            in_=hps[pslw][:], func=func,
                                            scale=scale_sb[:, o:o + 1]),
                                            deps=[ev_mm, ev_mul[pslw],
                                                  ev_scale])
                                        ev_mul[pslw] = em.op("dve", mk(
                                            "tensor_mul",
                                            a_sb[:, o * CH:(o + 1) * CH],
                                            ftmp[pslw][:],
                                            a_sb[:, o * CH:(o + 1) * CH]),
                                            deps=[ev_sl])
                                        ev_ev[o] = ev_sl
                                    else:
                                        ev_ev[o] = em.op("act", mk(
                                            "activation",
                                            out=a_sb[:, o * CH:(o + 1) * CH],
                                            in_=hps[pslw][:], func=func,
                                            scale=scale_sb[:, o:o + 1]),
                                            deps=[ev_mm, ev_scale])
                                return ev_mul if do_mult else ev_ev

                            updoor(w2_d, s2_sb, ev_s2, AF.Copy, False)
                            em.gate()
                            updoor(w1_d, s1_sb, ev_s1, AF.Silu, True)
                            em.gate()

                        with ExitStack() as bs:
                            b_sb = bs.enter_context(
                                SB("b_sb", [128, 32 * CH], BF16))
                            hcol = [bs.enter_context(
                                SB(f"hcol{i}", [128, 4096], BF16))
                                for i in range(3)]

                            def hid(w_dram, scale_sb, ev_scale, src, dst):
                                ev_hc = [None] * 3
                                ev_ev = {}
                                for o in range(32):
                                    slw = o % 3
                                    pslw = o % 2
                                    ev_hc[slw] = em.dma(
                                        "pl", hcol[slw][:],
                                        AP(w_dram, o * 4096,
                                           [[131072, 128], [1, 4096]]),
                                        deps=[ev_ev.get(o - 3)])
                                    ev_mm = None
                                    for i in range(32):
                                        for c in range(2):
                                            ev_mm = em.op("pe", mk(
                                                "matmul",
                                                hps[pslw][:, c * 512:
                                                          (c + 1) * 512],
                                                hcol[slw][:, i * 128:
                                                          (i + 1) * 128],
                                                src[:, i * CH + c * 512:
                                                    i * CH + (c + 1) * 512],
                                                start=(i == 0), stop=(i == 31)),
                                                deps=[ev_hc[slw],
                                                      ev_ev.get(o - 2)])
                                    ev_ev[o] = em.op("act", mk(
                                        "activation",
                                        out=dst[:, o * CH:(o + 1) * CH],
                                        in_=hps[pslw][:], func=AF.Gelu,
                                        scale=scale_sb[:, o:o + 1]),
                                        deps=[ev_mm, ev_scale])

                            hid(h1_d, sh1_sb, ev_sh1, a_sb, b_sb)
                            em.gate()
                            hid(h2_d, sh2_sb, ev_sh2, b_sb, a_sb)
                            em.gate()

                        with SB("w3h", [128, 32 * 512], BF16) as w3h, \
                             SB("w3hb", [128, 32 * 512], BF16) as w3hb, \
                             SB("otb0", [128, 512], F32) as otb0, \
                             SB("otb1", [128, 512], F32) as otb1, \
                             PS("w3p0", [128, 512], F32) as w3p0, \
                             PS("w3p1", [128, 512], F32) as w3p1:
                            w3hs = [w3h, w3hb]
                            otb = [otb0, otb1]
                            w3p = [w3p0, w3p1]
                            ev_w3mm_prev = None
                            ev_ot = [None, None]
                            ev_st2 = [None, None]
                            for half in range(2):
                                w3c = w3hs[half]
                                ev_w3l = em.dma(
                                    "pl", w3c[:],
                                    AP(w3_d, half * 32 * 512,
                                       [[2 * 32 * 512, 128], [1, 32 * 512]]),
                                    deps=[ev_w3mm_prev if ch > 0 and half == 0
                                          else None])
                                for t in range(nt2):
                                    slw = t % 2
                                    row0 = (base_t + t) * 128
                                    xp_ap = AP(out_d,
                                               sq * S * D + row0 * D
                                               + half * 512,
                                               [[D, 128], [1, 512]])
                                    ev_mm = None
                                    for i in range(32):
                                        ev_mm = em.op("pe", mk(
                                            "matmul", w3p[slw][:],
                                            a_sb[:, i * CH + t * 128:
                                                 i * CH + t * 128 + 128],
                                            w3c[:, i * 512:(i + 1) * 512],
                                            start=(i == 0), stop=(i == 31)),
                                            deps=[ev_w3l, ev_ot[slw]])
                                    ev_w3mm_prev = ev_mm
                                    ev_ot[slw] = em.op("dve", mk(
                                        "tensor_mul", otb[slw][:],
                                        w3p[slw][:],
                                        s3b_sb[:, half * 512:
                                               (half + 1) * 512]),
                                        deps=[ev_mm, ev_st2[slw]])
                                    ev_st2[slw] = em.dma(
                                        "pl", xp_ap, otb[slw][:],
                                        deps=[ev_ot[slw]], accum=True)
                                    final_store_evs.append(ev_st2[slw])
                    em.gate()

        bodies = em.finalize(sems_c, sems_d, final_waits=final_store_evs[-8:])

        with nc.Block() as blk:
            blk.tensor(bodies["pe"])
            blk.scalar(bodies["act"])
            blk.vector(bodies["dve"])
            blk.sync(bodies["sp"])
            blk.gpsimd(bodies["pl"])

    return nc


def _host_prep(inputs):
    x = np.ascontiguousarray(np.asarray(inputs["L0_attn_x"], dtype=np.float32))
    wq = np.asarray(inputs["L0_attn_wq"], np.float32)
    wk = np.asarray(inputs["L0_attn_wk"], np.float32)
    wv = np.asarray(inputs["L0_attn_wv"], np.float32)
    wo = np.asarray(inputs["L0_attn_wo"], np.float32)
    rb = np.ascontiguousarray(np.asarray(inputs["L0_attn_rel_bias"], np.float32))
    w1 = np.asarray(inputs["L0_ffn_w1_w"], np.float32)
    w2 = np.asarray(inputs["L0_ffn_w2_w"], np.float32)
    h1 = np.asarray(inputs["L0_ffn_h1_w"], np.float32)
    h2 = np.asarray(inputs["L0_ffn_h2_w"], np.float32)
    w3 = np.asarray(inputs["L0_ffn_w3_w"], np.float32)
    s1 = np.asarray(inputs["L0_ffn_w1_s"], np.float32)
    s2 = np.asarray(inputs["L0_ffn_w2_s"], np.float32)
    sh1 = np.asarray(inputs["L0_ffn_h1_s"], np.float32)
    sh2 = np.asarray(inputs["L0_ffn_h2_s"], np.float32)
    s3 = np.asarray(inputs["L0_ffn_w3_s"], np.float32)

    def qkvr(w):
        return np.ascontiguousarray(w.reshape(NFT, 128, D).transpose(1, 0, 2))

    wo_r = np.ascontiguousarray(
        wo.reshape(NFT, 128, D).transpose(1, 0, 2).reshape(128, NFT * D))

    def tile_io(wT, ni, no):
        return np.ascontiguousarray(
            wT.reshape(ni, 128, no, 128).transpose(1, 2, 0, 3))

    w1tl = tile_io(w1.T, 8, 32)
    w2tl = tile_io(w2.T, 8, 32)
    h1tl = tile_io(h1.T, 32, 32)
    h2tl = tile_io(h2.T, 32, 32)
    w3r = np.ascontiguousarray(
        w3.T.reshape(32, 128, 2, 512).transpose(1, 2, 0, 3))

    s1r = np.ascontiguousarray(s1.reshape(32, 128).T)
    s2r = np.ascontiguousarray(s2.reshape(32, 128).T)
    sh1r = np.ascontiguousarray(sh1.reshape(32, 128).T)
    sh2r = np.ascontiguousarray(sh2.reshape(32, 128).T)
    s3row = np.ascontiguousarray(s3.reshape(1, D))

    d = np.arange(-2047, 2048)
    n = -d
    ret = (n < 0).astype(np.int32) * 16
    na = np.abs(n)
    is_small = na < 8
    val_if_large = 8 + (np.log(np.maximum(na, 1).astype(np.float32) / 8.0)
                        / math.log(128.0 / 8.0) * 8.0).astype(np.int32)
    val_if_large = np.minimum(val_if_large, 15)
    bucket = ret + np.where(is_small, na, val_if_large)
    onehot = (bucket[None, :] == np.arange(32)[:, None]).astype(np.float32)

    ident = np.eye(128, dtype=np.float32)
    p = np.arange(128)
    indall = np.zeros((128, NFT * H), np.float32)
    for ft in range(NFT):
        indall[p < 64, ft * H + 2 * ft] = 1.0
        indall[p >= 64, ft * H + 2 * ft + 1] = 1.0

    shared = dict(wqr=qkvr(wq), wkr=qkvr(wk), wvr=qkvr(wv), wo_r=wo_r,
                  w1tl=w1tl, w2tl=w2tl, h1tl=h1tl, h2tl=h2tl, w3r=w3r,
                  s1r=s1r, s2r=s2r, sh1r=sh1r, sh2r=sh2r, s3row=s3row,
                  rb=rb, onehot=onehot, ident=ident, indall=indall)
    in_maps = []
    for c in range(8):
        m = dict(shared)
        m["x"] = np.ascontiguousarray(x[2 * c:2 * c + 2])
        in_maps.append(m)
    return in_maps


def kernel(**inputs):
    in_maps = _host_prep(inputs)
    nc = build_program()
    res = run_bass_kernel_spmd(nc, in_maps, core_ids=list(range(8)))
    return np.concatenate([r["out"] for r in res.results],
                          axis=0).reshape(16, S, D)
